# version 01 (final)
# baseline (speedup 1.0000x reference)
import math
import numpy as np
import ml_dtypes
import concourse.bass as bass
import concourse.mybir as mybir
from concourse.bass_utils import run_bass_kernel_spmd

F32 = mybir.dt.float32
F32R = mybir.dt.float32r
BF16 = mybir.dt.bfloat16
AF = mybir.ActivationFunctionType
ALU = mybir.AluOpType
AX = mybir.AxisListType

L, D, H, HD, CH = 4096, 2048, 16, 128, 512
NCH = L // CH
KD = D // 128
J = CH // 128
NS_A, NS_B, NS_C = 3.4445, -4.775, 2.0315
BASE_LR_INV = float(math.log(math.expm1(0.01)))

PHASES = ("ffn1", "attn", "ffn3")
DEBUG_TAPS = {}
N_CHUNKS = NCH
BUILD_LEVEL = 99
ATTN_CUT = "full"
HEAD_CUT = "full"

ENGS = ("te", "act", "dve", "sp", "gp")


class Res:
    def __init__(self):
        self.w = None
        self.readers = []

    def begin_write(self):
        ws = list(self.readers)
        if self.w is not None:
            ws.append(self.w)
        return ws

    def end_write(self, tok):
        if tok is not None:
            self.w = tok
        self.readers = []

    def begin_read(self):
        return [self.w] if self.w is not None else []

    def add_reader(self, tok):
        if tok is not None:
            self.readers.append(tok)


def later(*toks):
    best = {}
    for t in toks:
        if t is None:
            continue
        if isinstance(t, list):
            for tt in t:
                if tt is None:
                    continue
                e, v = tt
                if best.get(e, -1) < v:
                    best[e] = v
            continue
        e, v = t
        if best.get(e, -1) < v:
            best[e] = v
    return [(e, v) for e, v in best.items()]


class Bld:
    def __init__(self):
        self.prog = {e: [] for e in ENGS}
        self.n = {e: 0 for e in ENGS}
        self.lastw = {e: {} for e in ENGS}
        self.first_wait = {}

    def emit(self, eng, fn, waits=(), inc=False, amt=1, hard=()):
        items = self.prog[eng]
        if eng in self.first_wait and self.first_wait[eng] is not None:
            fw = self.first_wait[eng]
            self.first_wait[eng] = None
            waits = list(waits) + [fw]
        hardset = {t for t in later(*hard)}
        for t in later(*(list(waits) + list(hard))):
            pe, v = t
            if v <= 0:
                continue
            if pe == eng and eng not in ("sp", "gp") and t not in hardset:
                continue
            assert v <= self.n[pe], f"wait on future token {pe}:{v} have {self.n[pe]}"
            if self.lastw[eng].get(pe, 0) >= v:
                continue
            self.lastw[eng][pe] = v
            items.append(("w", pe, v))
        tok = None
        if inc:
            self.n[eng] += amt
            tok = (eng, self.n[eng])
        items.append(("o", fn, inc, amt))
        return tok

    def te(self, fn, waits=(), inc=False):
        return self.emit("te", fn, waits, inc)

    def act(self, fn, waits=(), inc=False, hard=()):
        return self.emit("act", fn, waits, inc, hard=hard)

    def dve(self, fn, waits=(), inc=False, hard=()):
        return self.emit("dve", fn, waits, inc, hard=hard)

    def dma(self, q, fn, waits=()):
        return self.emit(q, fn, waits, inc=True, amt=16)


def build_nc():
    nc = bass.Bass("TRN2", target_bir_lowering=False,
                   dynamic_dma_scratch_size=16384)

    def din(name, shape):
        return nc.dram_tensor(name, list(shape), F32, kind="ExternalInput").ap()

    xT = din("xT", (D, L))
    wqkv = din("wqkv", (D, 3 * D))
    wlr = din("wlr", (D, 3 * H))
    blr = din("blr", (3 * H,))
    wproj = din("wproj", (D, D))
    onw = din("onw", (HD,))
    w0T = din("w0T", (H, HD, HD))
    w2T = din("w2T", (H, HD, HD))
    w1T = din("w1T", (H, HD, HD))
    wffnT = din("wffnT", (D, D))
    bffn = din("bffn", (D,))
    lAT = din("lAT", (D, 128))
    lBT = din("lBT", (128, D))
    gwT = din("gwT", (D, 8))
    gb = din("gb", (8,))

    xout = nc.dram_tensor("xoutT", [D, L], F32, kind="ExternalOutput").ap()
    wqkv_bf = nc.dram_tensor("wqkv_bf", [D, 3 * D], BF16, kind="Internal").ap()
    wproj_bf = nc.dram_tensor("wproj_bf", [D, D], BF16, kind="Internal").ap()
    wffn_bf = nc.dram_tensor("wffn_bf", [D, D], BF16, kind="Internal").ap()

    taps = {name: nc.dram_tensor(f"tap_{name}", list(shape), F32,
                                 kind="ExternalOutput").ap()
            for name, shape in DEBUG_TAPS.items()}

    eyeF = np.eye(128, dtype=np.float32)
    c_identb = nc.inline_tensor(eyeF.astype(ml_dtypes.bfloat16), name="c_identb").ap()
    c_identf = nc.inline_tensor(eyeF, name="c_identf").ap()
    e3_np = np.tile(np.eye(16, dtype=np.float32)[None], (128, 1, 1))
    c_e3 = nc.inline_tensor(e3_np.astype(ml_dtypes.bfloat16), name="c_e3").ap()
    sel_np = np.zeros((8, 128), dtype=np.float32)
    for e in range(8):
        sel_np[e, 16 * e:16 * (e + 1)] = 1.0
    c_sel = nc.inline_tensor(sel_np, name="c_sel").ap()
    c_ones_r = nc.inline_tensor(np.ones((65, 128), dtype=np.float32),
                                name="c_ones_r").ap()
    c_ones_c = nc.inline_tensor(np.ones((128, 1), dtype=ml_dtypes.bfloat16),
                                name="c_ones_c").ap()
    idx_np = np.tile(np.arange(8, dtype=np.float32)[None, :], (128, 1))
    c_idx = nc.inline_tensor(idx_np, name="c_idx").ap()
    CONST_VALS = [1e-5, 1e-7, 1.0, -1.0, 0.5, -0.5, NS_A, NS_B, NS_C,
                  1.0 / 2048.0, 1.0 / 4096.0, 1.0 / 128.0, BASE_LR_INV,
                  -1e-5, 2048.0, -1e9, 1e9, -1e30, 1e30]
    ccols_np = np.tile(np.asarray(CONST_VALS, dtype=np.float32)[None, :],
                       (128, 1))
    c_ccols = nc.inline_tensor(ccols_np, name="c_ccols").ap()

    A = nc.alloc_sbuf_tensor
    xs = A("xs", [128, KD, CH], F32)
    ys = A("ys", [128, KD, CH], BF16)
    qkvs = A("qkvs", [128, 3 * H, CH], BF16)
    oins = A("oins", [128, H, CH], BF16)
    wbuf = [A(f"wbuf{i}", [128, KD, 256], BF16) for i in range(2)]
    lAT_sb = A("lAT_sb", [128, KD, 128], BF16)
    lBT_sb = A("lBT_sb", [128, D], BF16)
    gwT_sb = A("gwT_sb", [128, KD, 8], BF16)
    wlr_sb = A("wlr_sb", [128, KD, 48], BF16)
    halfb = A("halfb", [128, KD], F32)
    blr_c = A("blr_c", [48, 1], F32)
    gb_c = A("gb_c", [40, 1], F32)
    onw_c = A("onw_c", [128, 1], F32)
    z0m = A("z0m", [128, H, HD], F32)
    z2m = A("z2m", [128, H, HD], F32)
    y1m = A("y1m", [128, H, HD], F32)
    ws_b = A("ws_b", [128, H, 256], BF16)
    w1tn = A("w1tn", [128, H, HD], BF16)
    z1n = A("z1n", [128, H, HD], BF16)
    normc = A("normc", [128, H, 3], F32)
    ident = A("ident", [128, 128], BF16)
    identR = A("identR", [128, 128], F32R)
    ones_r = A("ones_r", [65, 128], F32R)
    ones_c = A("ones_c", [128, 1], BF16)
    e3b = A("e3b", [128, 16, 16], BF16)
    sel = A("sel", [8, 128], F32R)
    idxC = A("idxC", [128, 8], F32)
    sqp = A("sqp", [128, 2, CH], BF16)
    ccols = A("ccols", [128, 20], F32)
    rowsA = A("rowsA", [128, CH], F32R)
    rowsB = A("rowsB", [128, CH], F32R)
    rowsC = A("rowsC", [128, CH], F32R)
    lrrows = rowsA[0:48, :]
    rowscq = rowsA[64:80, :]
    rk_rows = rowsB[0:16, :]
    msr = rowsB[32:48, :]
    rq_rows = rowsB[64:80, :]
    g8 = rowsC[0:8, :]
    lfm = rowsC[32:40, :]
    rowC0 = rowsC[64:65, :]
    ltm = A("ltm", [128, 32], F32)
    l2t = A("l2t", [128, 32], F32)
    l3t = A("l3t", [128, 32], F32)
    msk1 = A("msk1", [128, 32], F32)
    gsc = A("gsc", [128, 24], F32)
    gwt = A("gwt", [128, 32], F32R)
    lrtm = A("lrtm", [128, J, 48], F32)
    rktm = A("rktm", [128, J, 16], F32)
    lr0p = A("lr0p", [128, J, 16], F32)
    lr2p = A("lr2p", [128, J, 16], F32)
    nsx = A("nsx", [128, 3, 128], BF16)
    nsz = A("nsz", [128, 3, 128], BF16)
    nsa_t = A("nsa_t", [128, 128], BF16)
    rsigr = A("rsigr", [1, 4], F32R)
    rsigc = A("rsigc", [128, 4], F32)
    junk = rsigc[:, 3:4]
    cn2 = A("cn2", [128, 8], F32)
    srow = cn2[0:1, :].bitcast(F32)
    scol = A("scol", [128, 3], F32)
    sqs = A("sqs", [128, 128], BF16)
    nsbm = sqs[:]
    stage_h = A("stage", [128, D], F32)
    stg2_h = A("stg2", [128, D], BF16)
    poolb_h = A("poolb", [128, 1536], BF16)
    stage = stage_h[:]
    stg2 = stg2_h[:]
    scr32 = stg2[:, 0:1024].bitcast(F32)
    poolb = poolb_h[:]
    sg_t = stage[:, 0:512].rearrange("p (j x) -> p j x", x=128)
    znew = stage[:, 0:384].rearrange("p (k x) -> p k x", x=128)
    ds_t = stage[:, 512:1024].rearrange("p (j x) -> p j x", x=128)
    hbk_t = stage[:, 1024:1536].rearrange("p (j x) -> p j x", x=128)
    dgt_t = stage[:, 1536:2048].rearrange("p (j x) -> p j x", x=128)
    d02_t = stg2[:, 0:1024].rearrange("p (j x) -> p j x", x=256)
    h1_t = stg2[:, 1024:1536].rearrange("p (j x) -> p j x", x=128)
    ktm_t = stg2[:, 1536:2048].rearrange("p (j x) -> p j x", x=128)
    qhat = poolb[:, 0:512]
    sgq_t = poolb[:, 512:1024]
    zq_t = poolb[:, 1024:1536]
    xa_sb = poolb[:, 0:512]
    xag = poolb[:, 512:1024]
    vtm_t = poolb[:, 1024:1536].rearrange("p (j x) -> p j x", x=128)

    PB = [nc.alloc_psum_tensor(f"pb{i}", [128, 512], F32) for i in range(8)]
    PBb = [p[:].bitcast(BF16) for p in PB]
    PBr = [p[:].bitcast(F32R) for p in PB]

    b = Bld()
    R = {}

    def res(name):
        if name not in R:
            R[name] = Res()
        return R[name]

    pb_r = [res(f"pb{i}") for i in range(8)]
    xs_r, ys_r = res("xs"), res("ys")
    wb_r = [res("wbuf0"), res("wbuf1")]
    stage_r, stg2_r = res("stage"), res("stg2")
    rowC0_r = res("rowC0")
    sqp_tok = [None] * 2

    def mm_acc(bank_ap, pairs, first_waits=(), per_waits=None):
        n = len(pairs)
        tok = None
        for i, (lh, rh) in enumerate(pairs):
            w = list(first_waits) if i == 0 else []
            if per_waits is not None and per_waits[i]:
                w = w + list(per_waits[i])
            tok = b.te(lambda t, lh=lh, rh=rh, i=i, n=n, ba=bank_ap:
                       t.matmul(ba, lh, rh, start=(i == 0), stop=(i == n - 1)),
                       waits=w, inc=(i == n - 1))
        return tok

    def tap(name, src_ap, wtoks):
        if name not in taps:
            return None
        return b.dma("gp", lambda g, src=src_ap: g.dma_start(out=taps[name][:],
                                                             in_=src),
                     waits=wtoks)

    def ld(dst_ap, src_ap, waits=()):
        return b.dma("gp", lambda g, d=dst_ap, s=src_ap: g.dma_start(out=d, in_=s),
                     waits=waits)

    ld(ident[:], c_identb[:])
    ld(identR[:], c_identf[:].bitcast(F32R))
    ld(ones_r[:], c_ones_r[:].bitcast(F32R))
    ld(ones_c[:], c_ones_c[:])
    ld(e3b[:], c_e3[:])
    ld(sel[:], c_sel[:].bitcast(F32R))
    ld(idxC[:], c_idx[:])
    ld(ccols[:, 0:19], c_ccols[:])
    t_const = ("gp", b.n["gp"])
    for ci, val in enumerate(CONST_VALS):
        nc.const_aps.register(F32, val, ccols[:, ci:ci + 1])
    b.first_wait = {"te": t_const, "act": t_const, "dve": t_const}

    if BUILD_LEVEL < 1:
        t_blr2 = t_bff2 = t_small_done = None
        t_states = []
    t_blr = ld(blr_c[:], blr.rearrange("(a b) -> a b", b=1)) if BUILD_LEVEL >= 1 else None
    t_gb = ld(gb_c[32:40, :], gb.rearrange("(a b) -> a b", b=1)) if BUILD_LEVEL >= 1 else None
    t_onw = ld(onw_c[:], onw.rearrange("(a b) -> a b", b=1)) if BUILD_LEVEL >= 1 else None
    t_bff = b.dma("gp", lambda g: g.dma_start(
        out=halfb[:], in_=bffn.rearrange("(i p) -> p i", p=128),
        allow_slow_non_contiguous=True)) if BUILD_LEVEL >= 1 else None
    if BUILD_LEVEL >= 1:
        b.act(lambda s: s.activation(blr_c[:], blr_c[:], AF.Identity, bias=BASE_LR_INV), waits=[t_blr], inc=True)
        t_blr2 = ("act", b.n["act"])
        t_bff2 = b.dve(lambda v: v.tensor_scalar_mul(halfb[:], halfb[:], 0.5),
                       waits=[t_bff], inc=True)
        t_z0 = ld(z0m[:], w0T.rearrange("h p d -> p h d"))
        t_z2 = ld(z2m[:], w2T.rearrange("h p d -> p h d"))
        t_y1 = ld(y1m[:], w1T.rearrange("h p d -> p h d"))
        t_states = later(t_z0, t_z2, t_y1)

    def ld_cast(dst_ap, src_ap, cols):
        ws = stage_r.begin_write()
        t1 = b.dma("gp", lambda g, d=src_ap, cols=cols: g.dma_start(
            out=stage[:, 0:cols], in_=d), waits=ws)
        stage_r.end_write(t1)
        rr = stage_r.begin_read()
        t2 = b.dve(lambda v, d=dst_ap, cols=cols: v.tensor_copy(d, stage[:, 0:cols]),
                   waits=rr, inc=True)
        stage_r.add_reader(t2)
        return t2

    if BUILD_LEVEL >= 1:
        for k in range(KD):
            ld_cast(lAT_sb[:, k, :], lAT[128 * k:128 * (k + 1), :], 128)
        for k in range(KD):
            ld_cast(gwT_sb[:, k, :], gwT[128 * k:128 * (k + 1), :], 8)
        for k in range(KD):
            ld_cast(wlr_sb[:, k, :], wlr[128 * k:128 * (k + 1), :], 48)
        t_small_done = ld_cast(lBT_sb[:], lBT[:, :], D)

    def prep_scratch(src, dst, ncols, scale_col=None):
        tout_last = None
        nblk = src.shape[0] // 128
        for blk in range(nblk):
            for c0 in range(0, ncols, D):
                cw = min(D, ncols - c0)
                ws = stage_r.begin_write()
                tin = b.dma("sp", lambda g, blk=blk, c0=c0, cw=cw, src=src:
                            g.dma_start(out=stage[:, 0:cw],
                                        in_=src[128 * blk:128 * (blk + 1),
                                                c0:c0 + cw]), waits=ws)
                stage_r.end_write(tin)
                ws2 = stage_r.begin_read() + stg2_r.begin_write()
                if scale_col is None:
                    tc = b.dve(lambda v, cw=cw: v.tensor_copy(stg2[:, 0:cw],
                                                              stage[:, 0:cw]),
                               waits=ws2, inc=True)
                else:
                    tc = b.dve(lambda v, cw=cw, sc=scale_col: v.tensor_scalar_mul(
                        stg2[:, 0:cw], stage[:, 0:cw], sc),
                        waits=ws2 + [t_onw], inc=True)
                stage_r.add_reader(tc)
                stg2_r.end_write(tc)
                rr = stg2_r.begin_read()
                tout = b.dma("gp", lambda g, blk=blk, c0=c0, cw=cw, dst=dst:
                             g.dma_start(out=dst[128 * blk:128 * (blk + 1),
                                                 c0:c0 + cw],
                                         in_=stg2[:, 0:cw]), waits=rr)
                stg2_r.add_reader(tout)
                tout_last = tout
        return tout_last

    if BUILD_LEVEL >= 2:
        t_wffn_ready = prep_scratch(wffnT, wffn_bf, D)
        t_wqkv_ready = prep_scratch(wqkv, wqkv_bf, 3 * D)
        t_wproj_ready = prep_scratch(wproj, wproj_bf, D, scale_col=onw_c[:])
    else:
        t_wffn_ready = t_wqkv_ready = t_wproj_ready = None

    tprev = None
    for h in range(H if BUILD_LEVEL >= 3 else 0):
        for k, m in enumerate((z0m, y1m, z2m)):
            ww = list(t_states) + ([tprev] if tprev else [])
            tq = b.act(lambda s, h=h, m=m: s.activation(
                lrtm[:].rearrange("p j x -> p (j x)")[:, 0:128], m[:, h, :],
                AF.Square, accum_out=cn2[:, 0:1]), waits=ww, inc=True)
            t1 = b.act(lambda s: s.activation(cn2[:, 4:5], cn2[:, 0:1], AF.Ln),
                       hard=[tq], inc=True)
            tprev = b.act(lambda s, h=h, k=k: s.activation(
                normc[:, h, k:k + 1], cn2[:, 4:5], AF.Exp, scale=0.5), inc=True)
    t_norm_done = tprev

    tw_last = None
    tt1 = None
    for h in range(H if BUILD_LEVEL >= 4 else 0):
        tc0 = b.act(lambda s, h=h: s.copy(nsx[:, 0, :], z0m[:, h, :]),
                    waits=t_states + [t_norm_done, t_const]
                    + ([tt1] if tt1 else []), inc=True)
        tc2 = b.act(lambda s, h=h: s.copy(nsx[:, 1, :], z2m[:, h, :]), inc=True)
        tc1 = b.act(lambda s, h=h: s.copy(w1tn[:, h, :], y1m[:, h, :]), inc=True)
        pw = pb_r[7].begin_write()
        b.te(lambda t: t.transpose(PBb[7][:, 0:128], nsx[:, 0, :], ident[:]),
             waits=later(tc0, pw))
        b.te(lambda t: t.transpose(PBb[7][:, 128:256], nsx[:, 1, :], ident[:]),
             waits=[tc2])
        tt1 = b.te(lambda t, h=h: t.transpose(PBb[7][:, 256:384], w1tn[:, h, :],
                                              ident[:]), waits=[tc1], inc=True)
        pb_r[7].end_write(tt1)
        rr = pb_r[7].begin_read()
        b.dve(lambda v, h=h: v.tensor_copy(ws_b[:, h, :], PBb[7][:, 0:256]),
              waits=rr)
        tw_last = b.dve(lambda v, h=h: v.tensor_copy(z1n[:, h, :],
                                                     PBb[7][:, 256:384]), inc=True)
        pb_r[7].add_reader(tw_last)
    t_state_ready = later(tw_last, t_norm_done, t_wproj_ready)
    tap("wsi", ws_b[:, 0, :], [tw_last])
    tap("w1i", w1tn[:, 0, :], [tw_last])
    tap("z1i", z1n[:, 0, :], [tw_last])
    tap("z0i", z0m[:, 0, :], [tw_last])
    tap("nc0", normc[:, 0, :], later(tw_last, t_norm_done))

    def load_x(c):
        ws = xs_r.begin_write()
        src = xT.rearrange("(i p) (c t) -> p i c t", p=128, t=CH)[:, :, c, :]
        t = b.dma("gp", lambda g, src=src: g.dma_start(out=xs[:], in_=src), waits=ws)
        xs_r.end_write(t)

    def store_x(c):
        rr = xs_r.begin_read()
        t = None
        for i in range(KD):
            t = b.dma("gp", lambda g, i=i, c=c: g.dma_start(
                out=xout[128 * i:128 * (i + 1), c * CH:(c + 1) * CH],
                in_=xs[:, i, :]), waits=rr if i == 0 else ())
        xs_r.add_reader(t)
        return t

    def emit_ln(c, tag):
        xw = xs_r.begin_read()
        w0 = pb_r[0].begin_write() + pb_r[1].begin_write()
        tmm = None
        for i in range(KD):
            sa = 0
            wv = later(xw, sqp_tok[0], sqp_tok[1])
            ta = b.act(lambda s, i=i, sa=sa: s.activation(sqp[:, sa, :], xs[:, i, :],
                                                          AF.Square),
                       waits=wv, inc=True)
            tb = b.act(lambda s, i=i: s.activation(sqp[:, 1, :], xs[:, i, :],
                                                   AF.Copy), inc=True)
            b.te(lambda t, i=i, sa=sa: t.matmul(PB[0][0:1, :], ones_c[:],
                                                sqp[:, sa, :], start=(i == 0),
                                                stop=(i == KD - 1)),
                 waits=later(ta, w0 if i == 0 else None,
                             t_const if i == 0 else None))
            tmm = b.te(lambda t, i=i: t.matmul(PB[1][0:1, :], ones_c[:],
                                               sqp[:, 1, :], start=(i == 0),
                                               stop=(i == KD - 1)),
                       waits=[tb], inc=True)
            sqp_tok[0] = tmm
            sqp_tok[1] = tmm
        pb_r[0].end_write(tmm)
        pb_r[1].end_write(tmm)
        xs_r.add_reader(tmm)
        wc0 = rowC0_r.begin_write()
        tmu_r = b.act(lambda s: s.activation(rowC0, PB[1][0:1, :], AF.Identity,
                                             scale=1.0 / 2048.0),
                      waits=later(tmm, wc0), inc=True)
        rowC0_r.end_write(tmu_r)
        pb_r[1].add_reader(tmu_r)
        vrow = scr32[0:1, :]
        tm2 = b.act(lambda s, vrow=vrow: s.activation(vrow, rowC0.bitcast(F32),
                                                      AF.Square), inc=True)
        rowC0_r.add_reader(tm2)
        tv = b.dve(lambda v, vrow=vrow: v.scalar_tensor_tensor(
            vrow, PB[0][0:1, :], 1.0 / 2048.0, vrow, ALU.mult, ALU.subtract),
            waits=[tm2, tmm], inc=True)
        pb_r[0].add_reader(tv)
        tvl = b.act(lambda s, vrow=vrow: s.activation(vrow, vrow, AF.Ln,
                                                      bias=1e-5),
                    waits=[tv], inc=True)
        wmu = pb_r[2].begin_write()
        tmu = b.te(lambda t: t.matmul(PB[2][:], ones_r[64:65, :], rowC0, start=True,
                                      stop=True), waits=later(tmu_r, wmu), inc=True)
        pb_r[2].end_write(tmu)
        rowC0_r.add_reader(tmu)
        wc0b = rowC0_r.begin_write()
        trs = b.act(lambda s, vrow=vrow: s.activation(rowC0, vrow, AF.Exp,
                                                      scale=-0.5),
                    waits=later(wc0b, tvl), inc=True)
        rowC0_r.end_write(trs)
        wrs = pb_r[3].begin_write()
        trsb = b.te(lambda t: t.matmul(PB[3][:], ones_r[64:65, :], rowC0, start=True,
                                       stop=True), waits=later(trs, wrs), inc=True)
        pb_r[3].end_write(trsb)
        rowC0_r.add_reader(trsb)
        yw = ys_r.begin_write()
        tlast = None
        for i in range(KD):
            b.dve(lambda v, i=i: v.tensor_sub(scr32[:], xs[:, i, :], PB[2][:]),
                  waits=later(tmu, trsb if i == 0 else None,
                              yw if i == 0 else None, tlast))
            tlast = b.dve(lambda v, i=i: v.tensor_mul(ys[:, i, :], scr32[:],
                                                      PB[3][:]),
                          waits=later(trsb), inc=True)
        pb_r[2].add_reader(tlast)
        pb_r[3].add_reader(tlast)
        xs_r.add_reader(tlast)
        ys_r.end_write(tlast)
        return tlast

    stream_state = {"buf": 0}

    def stream_group(src_bf, g, ready_tok):
        bi = stream_state["buf"]
        stream_state["buf"] = 1 - bi
        ws = wb_r[bi].begin_write() + later(ready_tok)
        src = src_bf.rearrange("(k p) (g o) -> p k g o", p=128, o=256)[:, :, g, :]
        t = b.dma("sp", lambda gg, src=src, bi=bi: gg.dma_start(out=wbuf[bi][:],
                                                                in_=src), waits=ws)
        wb_r[bi].end_write(t)
        return bi, t

    def emit_ffn(c, tag):
        yr = ys_r.begin_read()
        wl = pb_r[4].begin_write()
        tl = mm_acc(PB[4][32:40, :],
                    [(gwT_sb[:, k, :], ys[:, k, :]) for k in range(KD)],
                    first_waits=later(yr, wl, t_small_done))
        pb_r[4].end_write(tl)
        tlf = b.act(lambda s: s.activation(lfm, PB[4][32:40, :], AF.Identity,
                                           bias=gb_c[32:40, :]),
                    waits=later(pb_r[4].begin_read(), t_gb), inc=True)
        pb_r[4].add_reader(tlf)
        wp5 = pb_r[5].begin_write()
        tt = None
        for j in range(J):
            tt = b.te(lambda t, j=j: t.transpose(PBr[5][:, 8 * j:8 * (j + 1)],
                                                 lfm[:, 128 * j:128 * (j + 1)],
                                                 identR[32:40, 32:40]),
                      waits=later(tlf, wp5 if j == 0 else None), inc=(j == J - 1))
        pb_r[5].end_write(tt)
        tltm = b.dve(lambda v: v.tensor_copy(ltm[:], PBr[5][:, 0:32]),
                     waits=pb_r[5].begin_read(), inc=True)
        pb_r[5].add_reader(tltm)
        if c == 0 and tag == "f1":
            tap("ltm", ltm[:], [tltm])
        lg = ltm[:].rearrange("p (g e) -> p g e", e=8)
        l2g = l2t[:].rearrange("p (g e) -> p g e", e=8)
        l3g = l3t[:].rearrange("p (g e) -> p g e", e=8)
        mkg = msk1[:].rearrange("p (g e) -> p g e", e=8)
        gwg = gwt[:].rearrange("p (g e) -> p g e", e=8)
        idxb = idxC[:].unsqueeze(1).broadcast_to([128, 4, 8])

        def bc(col4):
            return col4.unsqueeze(-1).broadcast_to([128, 4, 8])

        t0 = b.dve(lambda v: v.tensor_reduce(gsc[:, 0:4], lg, AX.X, ALU.max),
                   hard=[tltm], inc=True)
        t1 = b.dve(lambda v: v.tensor_tensor(mkg, lg, bc(gsc[:, 0:4]),
                                             ALU.is_ge), hard=[t0], inc=True)
        t2 = b.dve(lambda v: v.tensor_scalar(l2t[:], msk1[:], -1e9, 1e9,
                                             ALU.mult, ALU.add),
                   hard=[t1], inc=True)
        t3 = b.dve(lambda v: v.tensor_tensor(l2g, l2g, idxb, ALU.add),
                   hard=[t2], inc=True)
        t4 = b.dve(lambda v: v.tensor_reduce(gsc[:, 16:20], l2g, AX.X,
                                             ALU.min), hard=[t3], inc=True)
        t5 = b.dve(lambda v: v.tensor_tensor(mkg, l2g, bc(gsc[:, 16:20]),
                                             ALU.is_le), hard=[t4], inc=True)
        t6 = b.dve(lambda v: v.scalar_tensor_tensor(l2t[:], msk1[:], -1e30,
                                                    ltm[:], ALU.mult, ALU.add),
                   hard=[t5], inc=True)
        t7 = b.dve(lambda v: v.tensor_reduce(gsc[:, 4:8], l2g, AX.X, ALU.max),
                   hard=[t6], inc=True)
        t8 = b.dve(lambda v: v.tensor_tensor(l3g, l2g, bc(gsc[:, 4:8]),
                                             ALU.is_ge), hard=[t7], inc=True)
        t9 = b.dve(lambda v: v.tensor_scalar(l3t[:], l3t[:], -1e9, 1e9,
                                             ALU.mult, ALU.add),
                   hard=[t8], inc=True)
        t10 = b.dve(lambda v: v.tensor_tensor(l3g, l3g, idxb, ALU.add),
                    hard=[t9], inc=True)
        t11 = b.dve(lambda v: v.tensor_reduce(gsc[:, 20:24], l3g, AX.X,
                                              ALU.min), hard=[t10], inc=True)
        trm = b.dve(lambda v: v.tensor_tensor(l3g, l3g, bc(gsc[:, 20:24]),
                                              ALU.is_le), hard=[t11], inc=True)
        td = b.dve(lambda v: v.tensor_sub(gsc[:, 8:12], gsc[:, 0:4],
                                          gsc[:, 4:8]), inc=True)
        tsig = b.act(lambda s: s.activation(gsc[:, 8:12], gsc[:, 8:12],
                                            AF.Sigmoid), waits=[td], inc=True)
        tg2 = b.dve(lambda v: v.tensor_scalar(gsc[:, 12:16], gsc[:, 8:12],
                                              -1.0, 1.0, ALU.mult, ALU.add),
                    waits=[tsig], inc=True)
        tc1 = b.dve(lambda v: v.tensor_tensor(gwg, mkg, bc(gsc[:, 8:12]),
                                              ALU.mult), inc=True)
        tc2 = b.dve(lambda v: v.tensor_tensor(
            ltm[:].rearrange("p (g e) -> p g e", e=8), l3g, bc(gsc[:, 12:16]),
            ALU.mult), hard=[tg2], inc=True)
        tw = b.dve(lambda v: v.tensor_add(gwt[:], gwt[:], ltm[:]),
                   hard=[tc2], inc=True)
        if c == 0 and tag == "f1":
            tap("gwt", gwt[:].bitcast(F32), [tw])
            tap("gsc", gsc[:], [tw])
            tap("msk", msk1[:], [tw])
            tap("l3t", l3t[:], [tw])
            tap("l2t", l2t[:], [tw])
        w5 = pb_r[5].begin_write()
        tt2 = None
        for j in range(J):
            tt2 = b.te(lambda t, j=j: t.transpose(
                PBr[5][0:8, 128 * j:128 * (j + 1)],
                gwt[:, 8 * j:8 * (j + 1)], identR[:]),
                       waits=later(tw, w5 if j == 0 else None), inc=(j == J - 1))
        pb_r[5].end_write(tt2)
        tg8 = b.dve(lambda v: v.tensor_copy(g8, PBr[5][0:8, :]),
                    waits=pb_r[5].begin_read(), inc=True)
        pb_r[5].add_reader(tg8)
        w6 = pb_r[6].begin_write()
        tge = b.te(lambda t: t.matmul(PB[6][:], sel[0:8, :], g8, start=True, stop=True),
                   waits=later(tg8, w6), inc=True)
        pb_r[6].end_write(tge)
        w4 = pb_r[4].begin_write()
        txa = mm_acc(PB[4][:], [(lAT_sb[:, k, :], ys[:, k, :]) for k in range(KD)],
                     first_waits=later(yr, w4))
        pb_r[4].end_write(txa)
        txe = b.act(lambda s: s.copy(xa_sb, PB[4][:]),
                    waits=pb_r[4].begin_read(), inc=True)
        pb_r[4].add_reader(txe)
        r6 = pb_r[6].begin_read()
        txg = b.dve(lambda v: v.tensor_mul(xag, xa_sb, PB[6][:]),
                    waits=later(txe, r6), inc=True)
        pb_r[6].add_reader(txg)
        if c == 0 and tag == "f1" and "xag" in taps:
            tap("xag", xag, [txg])
        if c == 0 and tag == "f1" and "xa" in taps:
            tap("xa", xa_sb, [txe, txg])
        wsx = xs_r.begin_write()
        tlast = None
        bi, twg = None, None
        for ob in range(KD):
            g, half = divmod(ob, 2)
            if half == 0:
                bi, twg = stream_group(
                    wffn_bf, g, [t_wffn_ready] if c == 0 else None)
            bank = ob % 2
            wbk = pb_r[bank].begin_write()
            pairs = [(wbuf[bi][:, k, 128 * half:128 * (half + 1)], ys[:, k, :])
                     for k in range(KD)]
            pairs.append((lBT_sb[:, 128 * ob:128 * (ob + 1)], xag))
            pw = [None] * KD + [later(txg)]
            tmm = mm_acc(PB[bank][:], pairs,
                         first_waits=later(twg, wbk, yr), per_waits=pw)
            pb_r[bank].end_write(tmm)
            wb_r[bi].add_reader(tmm)
            tev = b.act(lambda s, ob=ob, bank=bank: s.activation(
                scr32[:], PB[bank][:], AF.Identity, scale=0.5,
                bias=halfb[:, ob:ob + 1]),
                waits=later(pb_r[bank].begin_read(), t_bff2, tlast), inc=True)
            pb_r[bank].add_reader(tev)
            tlast = b.dve(lambda v, ob=ob: v.tensor_add(xs[:, ob, :], xs[:, ob, :],
                                                        scr32[:]),
                          waits=later(tev, wsx if ob == 0 else None), inc=True)
        xs_r.end_write(tlast)
        ys_r.add_reader(tlast)
        return tlast

    def emit_lrqkv(c):
        yr = ys_r.begin_read()
        w4 = pb_r[4].begin_write()
        tl = mm_acc(PB[4][0:48, :],
                    [(wlr_sb[:, k, :], ys[:, k, :]) for k in range(KD)],
                    first_waits=later(yr, w4, t_small_done))
        pb_r[4].end_write(tl)
        te1 = b.act(lambda s: s.activation(lrrows, PB[4][0:48, :], AF.Exp,
                                           bias=blr_c[:]),
                    waits=later(pb_r[4].begin_read(), t_blr2), inc=True)
        pb_r[4].add_reader(te1)
        tsp = b.act(lambda s: s.activation(lrrows, lrrows, AF.Ln, bias=1.0),
                    inc=True)
        if c == 0:
            tap("lr", lrrows.bitcast(F32), [tsp])
        w5 = pb_r[5].begin_write()
        ttr = None
        for j in range(J):
            ttr = b.te(lambda t, j=j: t.transpose(PBr[5][:, 48 * j:48 * (j + 1)],
                                                  lrrows[:, 128 * j:128 * (j + 1)],
                                                  identR[0:48, 0:48]),
                       waits=later(tsp, w5 if j == 0 else None), inc=(j == J - 1))
        pb_r[5].end_write(ttr)
        tlrtm = b.dve(lambda v: v.tensor_copy(
            lrtm[:].rearrange("p j x -> p (j x)"), PBr[5][:, 0:192]),
            waits=pb_r[5].begin_read(), inc=True)
        pb_r[5].add_reader(tlrtm)
        w6 = pb_r[6].begin_write()
        w7q = pb_r[7].begin_write()
        tq2 = None
        tk2 = None
        tev_by_jcol = {}
        for g in range(3 * D // 256):
            bi, twg = stream_group(wqkv_bf, g,
                                   [t_wqkv_ready] if c == 0 else None)
            for half in (0, 1):
                jcol = 2 * g + half
                bank = jcol % 2
                wbk = pb_r[bank].begin_write()
                tmm = mm_acc(PB[bank][:],
                             [(wbuf[bi][:, k, 128 * half:128 * (half + 1)],
                               ys[:, k, :]) for k in range(KD)],
                             first_waits=later(twg, wbk, yr))
                pb_r[bank].end_write(tmm)
                wb_r[bi].add_reader(tmm)
                tev = b.act(lambda s, jcol=jcol, bank=bank: s.activation(
                    qkvs[:, jcol, :], PB[bank][:], AF.Silu),
                    waits=pb_r[bank].begin_read(), inc=True)
                pb_r[bank].add_reader(tev)
                tev_by_jcol[jcol] = tev
                if jcol < 32:
                    slot = jcol % 2
                    tsq = b.act(lambda s, jcol=jcol, slot=slot: s.activation(
                        sqp[:, slot, :], qkvs[:, jcol, :], AF.Square),
                        waits=later(sqp_tok[slot]), inc=True)
                    if jcol < 16:
                        tq2 = b.te(lambda t, jcol=jcol, slot=slot: t.matmul(
                            PB[6][64:80, :], e3b[:, jcol, :], sqp[:, slot, :],
                            start=(jcol == 0), stop=(jcol == 15)),
                            waits=later(tsq, w6 if jcol == 0 else None),
                            inc=(jcol == 15))
                        sqp_tok[slot] = tq2
                    else:
                        tk2 = b.te(lambda t, jcol=jcol, slot=slot: t.matmul(
                            PB[7][0:16, :], e3b[:, jcol - 16, :], sqp[:, slot, :],
                            start=(jcol == 16), stop=(jcol == 31)),
                            waits=later(tsq, w7q if jcol == 16 else None),
                            inc=(jcol == 31))
                        sqp_tok[slot] = tk2
        pb_r[6].end_write(tq2)
        pb_r[7].end_write(tk2)
        tr1 = b.act(lambda s: s.activation(rowscq, PB[6][64:80, :], AF.Ln),
                    waits=pb_r[6].begin_read(), inc=True)
        pb_r[6].add_reader(tr1)
        tr2 = b.act(lambda s: s.activation(rowscq, rowscq, AF.Exp, scale=-0.5),
                    inc=True)
        tr3 = b.dve(lambda v: v.tensor_scalar(scr32[64:80, :], rowscq, -1e-5, 1.0,
                                              ALU.mult, ALU.add),
                    waits=later(tr2), inc=True)
        trqq = b.dve(lambda v: v.tensor_mul(rq_rows, rowscq, scr32[64:80, :]),
                     waits=[tr3], inc=True)
        if c == 0:
            tap("rq", rq_rows.bitcast(F32), [trqq])
        tk1 = b.act(lambda s: s.activation(lrrows[0:16, :], PB[7][0:16, :], AF.Ln),
                    waits=later(pb_r[7].begin_read(), tlrtm), inc=True)
        pb_r[7].add_reader(tk1)
        tk2b = b.act(lambda s: s.activation(lrrows[0:16, :], lrrows[0:16, :],
                                            AF.Exp, scale=-0.5), inc=True)
        tk3 = b.dve(lambda v: v.tensor_scalar(scr32[0:16, :], lrrows[0:16, :],
                                              -1e-5, 1.0, ALU.mult, ALU.add),
                    waits=later(tk2b), inc=True)
        trq = b.dve(lambda v: v.tensor_mul(rk_rows, lrrows[0:16, :],
                                           scr32[0:16, :]),
                    waits=[tk3], inc=True)
        if c == 0:
            tap("rk", rk_rows.bitcast(F32), [trq])
        w5b = pb_r[5].begin_write()
        ttk = None
        for j in range(J):
            ttk = b.te(lambda t, j=j: t.transpose(
                PBr[5][:, 192 + 16 * j:192 + 16 * (j + 1)],
                rk_rows[:, 128 * j:128 * (j + 1)], identR[0:16, 0:16]),
                waits=later(trq, w5b if j == 0 else None), inc=(j == J - 1))
        pb_r[5].end_write(ttk)
        trk = b.dve(lambda v: v.tensor_copy(
            rktm[:].rearrange("p j x -> p (j x)"), PBr[5][:, 192:256]),
            waits=pb_r[5].begin_read(), inc=True)
        pb_r[5].add_reader(trk)
        b.dve(lambda v: v.tensor_mul(lr0p[:], lrtm[:, :, 0:16], rktm[:]),
              waits=later(tlrtm), hard=[trk])
        tlp = b.dve(lambda v: v.tensor_mul(lr2p[:], lrtm[:, :, 32:48], rktm[:]),
                    inc=True)
        if c == 0:
            tap("lr0p", lr0p[:].rearrange("p j x -> p (j x)"), [tlp])
        return {"lr": tlp, "lrtm": tlrtm, "qkv": tev_by_jcol, "rq": trqq}

    def emit_head(c, h, qtoks, prev_tok=None):
        tev = qtoks["qkv"]
        w0_ = pb_r[0].begin_write()
        w1_ = pb_r[1].begin_write()
        w2_ = pb_r[2].begin_write()
        tg = th = tdh = None
        for j in range(J):
            tg = b.te(lambda t, j=j, h=h: t.matmul(
                PB[0][:, 128 * j:128 * (j + 1)],
                qkvs[:, 16 + h, 128 * j:128 * (j + 1)], ws_b[:, h, 0:128],
                start=True, stop=True),
                waits=later(tev[16 + h] if j == 0 else None,
                            w0_ if j == 0 else None,
                            prev_tok if j == 0 else None,
                            t_state_ready if (c == 0 and h == 0 and j == 0)
                            else None),
                inc=(j == J - 1))
            th = b.te(lambda t, j=j, h=h: t.matmul(
                PB[1][:, 128 * j:128 * (j + 1)],
                qkvs[:, 16 + h, 128 * j:128 * (j + 1)], ws_b[:, h, 128:256],
                start=True, stop=True),
                waits=later(w1_ if j == 0 else None), inc=(j == J - 1))
            tdh = b.te(lambda t, j=j, h=h: t.matmul(
                PB[2][:, 128 * j:128 * (j + 1)],
                qkvs[:, 32 + h, 128 * j:128 * (j + 1)], w1tn[:, h, :],
                start=True, stop=True),
                waits=later(tev[32 + h] if j == 0 else None,
                            w2_ if j == 0 else None),
                inc=(j == J - 1))
        pb_r[0].end_write(tg)
        pb_r[1].end_write(th)
        pb_r[2].end_write(tdh)
        w3 = pb_r[3].begin_write()
        ttv = None
        for j in range(J):
            b.te(lambda t, j=j, h=h: t.transpose(
                PBb[3][:, 128 * j:128 * (j + 1)],
                qkvs[:, 16 + h, 128 * j:128 * (j + 1)], ident[:]),
                waits=later(w3 if j == 0 else None))
            ttv = b.te(lambda t, j=j, h=h: t.transpose(
                PBb[3][:, 512 + 128 * j:512 + 128 * (j + 1)],
                qkvs[:, 32 + h, 128 * j:128 * (j + 1)], ident[:]),
                inc=(j == J - 1))
        pb_r[3].end_write(ttv)
        r3 = pb_r[3].begin_read()
        tke = b.dve(lambda v: v.tensor_copy(stg2[:, 1536:2048], PBb[3][:, 0:512]),
                    waits=r3, inc=True)
        pb_r[3].add_reader(tke)
        tve = b.dve(lambda v: v.tensor_copy(
            vtm_t[:].rearrange("p j x -> p (j x)"), PBb[3][:, 512:1024]),
            inc=True)
        pb_r[3].add_reader(tve)
        rg = pb_r[0].begin_read()
        rh = pb_r[1].begin_read()
        rd = pb_r[2].begin_read()
        tsg = tds = thb = None
        for j in range(J):
            tsg = b.act(lambda s, j=j, h=h: s.activation(
                sg_t[:, j, :], PB[0][:, 128 * j:128 * (j + 1)], AF.Silu,
                scale=rktm[:, j, h:h + 1]),
                waits=later(rg if j == 0 else None,
                            qtoks["lr"] if j == 0 else None,
                            tg if j == 0 else None),
                inc=(j == J - 1))
        for j in range(J):
            tds = b.act(lambda s, j=j, h=h: s.activation(
                ds_t[:, j, :], PB[0][:, 128 * j:128 * (j + 1)],
                AF.Derivative_silu, scale=rktm[:, j, h:h + 1]), inc=(j == J - 1))
        for j in range(J):
            thb = b.act(lambda s, j=j, h=h: s.activation(
                hbk_t[:, j, :], PB[1][:, 128 * j:128 * (j + 1)], AF.Identity,
                scale=rktm[:, j, h:h + 1]),
                waits=later(rh if j == 0 else None, th if j == 0 else None),
                inc=(j == J - 1))
        pb_r[0].add_reader(tds)
        pb_r[1].add_reader(thb)
        if c == 0 and h == 0:
            tap("sg0", sg_t[:].rearrange("p j x -> p (j x)"), [tsg])
            tap("hb0", hbk_t[:].rearrange("p j x -> p (j x)"), [thb])
            tap("ktm0", ktm_t[:].rearrange("p j x -> p (j x)"), [tke])
            tap("vtm0", vtm_t[:].rearrange("p j x -> p (j x)"), [tve])
        tdg = b.dve(lambda v: v.tensor_tensor(
            dgt_t[:].rearrange("p j x -> p (j x)"), PB[2][:],
            hbk_t[:].rearrange("p j x -> p (j x)"), ALU.mult),
            waits=later(rd, tdh, thb), inc=True)
        t_h1 = t_d0 = t_d2 = None
        for j in range(J):
            t_h1 = b.dve(lambda v, j=j, h=h: v.scalar_tensor_tensor(
                h1_t[:, j, :], sg_t[:, j, :], lrtm[:, j, 16 + h:17 + h],
                hbk_t[:, j, :], ALU.mult, ALU.mult),
                waits=later(tsg if j == 0 else None), inc=(j == J - 1),
                hard=[qtoks["lrtm"]] if (j == 0 and h == 0) else ())
        for j in range(J):
            t_d0 = b.dve(lambda v, j=j, h=h: v.scalar_tensor_tensor(
                d02_t[:, j, 0:128], dgt_t[:, j, :], lr0p[:, j, h:h + 1],
                ds_t[:, j, :], ALU.mult, ALU.mult),
                waits=later(tds if j == 0 else None, tdg if j == 0 else None),
                inc=(j == J - 1),
                hard=[qtoks["lr"]] if (j == 0 and h == 0) else ())
        for j in range(J):
            t_d2 = b.dve(lambda v, j=j, h=h: v.scalar_tensor_tensor(
                d02_t[:, j, 128:256], PB[2][:, 128 * j:128 * (j + 1)],
                lr2p[:, j, h:h + 1], sg_t[:, j, :], ALU.mult, ALU.mult),
                inc=(j == J - 1))
        pb_r[2].add_reader(t_d2)
        t_tapd = None
        if c == 0 and h == 0:
            tap("dg0", dgt_t[:].rearrange("p j x -> p (j x)"), [tdg])
            tap("d020", d02_t[:].rearrange("p j x -> p (j x)"), [t_d2])
            t_tapd = tap("h10", h1_t[:].rearrange("p j x -> p (j x)"), [t_h1])
        w4 = pb_r[4].begin_write()
        w5 = pb_r[5].begin_write()
        tg02 = mm_acc(PB[4][:, 0:256],
                      [(ktm_t[:, j, :], d02_t[:, j, :]) for j in range(J)],
                      first_waits=later(tke, t_d0, t_d2, w4))
        tg0t = mm_acc(PB[4][:, 256:384],
                      [(d02_t[:, j, 0:128], ktm_t[:, j, :]) for j in range(J)])
        tg2t = mm_acc(PB[4][:, 384:512],
                      [(d02_t[:, j, 128:256], ktm_t[:, j, :]) for j in range(J)])
        tg1t = mm_acc(PB[5][:, 0:128],
                      [(vtm_t[:, j, :], h1_t[:, j, :]) for j in range(J)],
                      first_waits=later(tve, t_h1, w5))
        tg1 = mm_acc(PB[5][:, 128:256],
                     [(h1_t[:, j, :], vtm_t[:, j, :]) for j in range(J)])
        pb_r[4].end_write(tg2t)
        pb_r[5].end_write(tg1)
        if HEAD_CUT == "fwd":
            pb_r[4].add_reader(None)
            tdum = b.dve(lambda v: v.tensor_copy(junk[:], junk[:]),
                         waits=later(tg2t, tg1), inc=True)
            return None, tdum
        r4 = pb_r[4].begin_read()
        r5 = pb_r[5].begin_read()
        tx = [None] * 3
        tz = [None] * 3
        tx[0] = b.act(lambda s: s.copy(nsx[:, 0, :], PB[4][:, 0:128]),
                      waits=later(r4, tg02), inc=True)
        tx[2] = b.act(lambda s: s.copy(nsx[:, 2, :], PB[4][:, 128:256]),
                      inc=True)
        tz[0] = b.act(lambda s: s.copy(nsz[:, 0, :], PB[4][:, 256:384]),
                      waits=later(tg0t), inc=True)
        tz[2] = b.act(lambda s: s.copy(nsz[:, 2, :], PB[4][:, 384:512]),
                      waits=later(tg2t), inc=True)
        tx[1] = b.act(lambda s: s.copy(nsx[:, 1, :], PB[5][:, 0:128]),
                      waits=later(r5, tg1t), inc=True)
        tz[1] = b.act(lambda s: s.copy(nsz[:, 1, :], PB[5][:, 128:256]),
                      waits=later(tg1), inc=True)
        pb_r[4].add_reader(tx[0])
        pb_r[4].add_reader(tz[0])
        pb_r[4].add_reader(tz[2])
        pb_r[4].add_reader(tx[2])
        pb_r[5].add_reader(tx[1])
        pb_r[5].add_reader(tz[1])
        if HEAD_CUT.startswith("evac"):
            tdum = b.dve(lambda v: v.tensor_copy(junk[:], junk[:]),
                         waits=later(tg2t, tg1, tx[0], tx[1], tx[2], tz[0],
                                     tz[1], tz[2]),
                         inc=True)
            return None, tdum
        w5s = pb_r[5].begin_write()
        tsig = []
        for k in range(3):
            tq = b.act(lambda s, k=k: s.activation(sqs[:], nsx[:, k, :],
                                                   AF.Square),
                       waits=later(tx[k], tsig[-1] if tsig else None), inc=True)
            tm = b.te(lambda t, k=k: t.matmul(PB[5][0:1, 256:384],
                                              ones_c[:], sqs[:],
                                              start=True, stop=True),
                      waits=later(tq, w5s if k == 0 else None,
                                  tsig[-1] if tsig else None), inc=True)
            trd = b.dve(lambda v, k=k: v.tensor_reduce(srow[0:1, k:k + 1],
                                                       PB[5][0:1, 256:384],
                                                       AX.X, ALU.add),
                        waits=[tm], inc=True)
            tsig.append(trd)
        pb_r[5].end_write(tsig[-1])
        pb_r[5].add_reader(tsig[-1])
        ta = b.act(lambda s: s.activation(srow[0:1, 4:8], srow[0:1, 0:4], AF.Ln),
                   waits=[tsig[-1]], inc=True)
        tb = b.act(lambda s: s.activation(srow[0:1, 4:8], srow[0:1, 4:8], AF.Exp,
                                          scale=0.5), inc=True)
        tc = b.dve(lambda v: v.tensor_scalar_add(srow[0:1, 4:8], srow[0:1, 4:8],
                                                 1e-7), waits=[tb], inc=True)
        b.act(lambda s: s.activation(srow[0:1, 4:8], srow[0:1, 4:8], AF.Ln),
              waits=[tc])
        td = b.act(lambda s: s.activation(rsigr[0:1, 0:4], srow[0:1, 4:8],
                                          AF.Exp, scale=-1.0), inc=True)
        w5c = pb_r[5].begin_write()
        tbc = b.te(lambda t: t.matmul(PB[5][:, 384:388], ones_r[0:1, :],
                                      rsigr[0:1, 0:4], start=True, stop=True),
                   waits=later(td, w5c), inc=True)
        pb_r[5].end_write(tbc)
        tce = b.dve(lambda v: v.tensor_copy(rsigc[:, 0:3], PB[5][:, 384:387]),
                    waits=pb_r[5].begin_read(), inc=True)
        pb_r[5].add_reader(tce)
        if HEAD_CUT == "sig":
            return None, tce
        tscale = []
        for k in range(3):
            b.dve(lambda v, k=k: v.tensor_scalar_mul(nsx[:, k, :], nsx[:, k, :],
                                                     rsigc[:, k:k + 1]),
                  hard=[tce] if k == 0 else ())
            ts2 = b.dve(lambda v, k=k: v.tensor_scalar_mul(nsz[:, k, :],
                                                           nsz[:, k, :],
                                                           rsigc[:, k:k + 1]),
                        waits=later(tz[k]), inc=True)
            tscale.append(ts2)
        if HEAD_CUT == "scale":
            tdum = b.dve(lambda v: v.tensor_copy(junk[:], junk[:]),
                         waits=later(*tscale), inc=True)
            return None, tdum
        t_tapn = None
        if c == 0 and h == 0:
            tap("nsxs", nsx[:].rearrange("p j x -> p (j x)"),
                later(*tscale))
            t_tapn = tap("nszs", nsz[:].rearrange("p j x -> p (j x)"),
                         later(*tscale))
        tns = [None] * 3
        for k in range(3):
            tY = tscale[k]
            for it in range(2):
                w4n = pb_r[4].begin_write()
                ta_ = b.te(lambda t, k=k: t.matmul(PB[4][:, 0:128], nsz[:, k, :],
                                                   nsz[:, k, :], start=True,
                                                   stop=True),
                           waits=later(tY, w4n,
                                       t_tapn if (k == 0 and it == 0)
                                       else None), inc=True)
                pb_r[4].end_write(ta_)
                tae = b.dve(lambda v: v.tensor_copy(nsa_t[:], PB[4][:, 0:128]),
                            waits=pb_r[4].begin_read(), inc=True)
                pb_r[4].add_reader(tae)
                w5n = pb_r[5].begin_write()
                ta2 = b.te(lambda t: t.matmul(PB[5][:, 0:128], nsa_t[:], nsa_t[:],
                                              start=True, stop=True),
                           waits=later(tae, w5n), inc=True)
                pb_r[5].end_write(ta2)
                tba = b.dve(lambda v: v.tensor_scalar_mul(nsbm, nsa_t[:], NS_B),
                            hard=[tae], inc=True)
                tbm = b.dve(lambda v: v.scalar_tensor_tensor(
                    nsbm, PB[5][:, 0:128], NS_C, nsbm, ALU.mult, ALU.add),
                    waits=later(pb_r[5].begin_read()), hard=[tba], inc=True)
                pb_r[5].add_reader(tbm)
                w4u = pb_r[4].begin_write()
                tu = b.te(lambda t, k=k: t.matmul(PB[4][:, 128:256], nsbm,
                                                  nsx[:, k, :], start=True,
                                                  stop=True),
                          waits=later(tbm, w4u), inc=True)
                pb_r[4].end_write(tu)
                w5v = pb_r[5].begin_write()
                tv = b.te(lambda t, k=k: t.matmul(PB[5][:, 128:256], nsx[:, k, :],
                                                  nsbm, start=True, stop=True),
                          waits=later(w5v), inc=True)
                pb_r[5].end_write(tv)
                ty2 = b.dve(lambda v, k=k: v.scalar_tensor_tensor(
                    nsx[:, k, :], nsx[:, k, :], NS_A, PB[4][:, 128:256],
                    ALU.mult, ALU.add),
                    waits=later(tu, pb_r[4].begin_read()), inc=True)
                pb_r[4].add_reader(ty2)
                tz2_ = b.dve(lambda v, k=k: v.scalar_tensor_tensor(
                    nsz[:, k, :], nsz[:, k, :], NS_A, PB[5][:, 128:256],
                    ALU.mult, ALU.add),
                    waits=later(tv, pb_r[5].begin_read()), inc=True)
                pb_r[5].add_reader(tz2_)
                tY = later(ty2, tz2_)
            tns[k] = tY
        if HEAD_CUT == "ns":
            tdum = b.dve(lambda v: v.tensor_copy(junk[:], junk[:]),
                         waits=later(*tns), inc=True)
            return None, tdum
        t_tapp = None
        if c == 0 and h == 0:
            tap("nsxp", nsx[:].rearrange("p j x -> p (j x)"), later(*tns))
            t_tapp = tap("nszp", nsz[:].rearrange("p j x -> p (j x)"),
                         later(*tns))
        tup = []
        for k, (mast, nsside) in enumerate(((z0m, nsz), (y1m, nsx), (z2m, nsz))):
            tadd = b.dve(lambda v, k=k, mast=mast, nsside=nsside, h=h:
                         v.tensor_add(znew[:, k, :], mast[:, h, :],
                                      nsside[:, k, :]),
                         waits=later(tns[k]), inc=True)
            tsq = b.act(lambda s, k=k: s.activation(
                sqp[:, 0, 0:128], znew[:, k, :], AF.Square,
                accum_out=cn2[:, k:k + 1]),
                waits=later([tadd], sqp_tok[0] if k == 0 else None), inc=True)
            tup.append(tsq)
        t1 = b.act(lambda s: s.activation(cn2[:, 4:7], cn2[:, 0:3], AF.Ln),
                   hard=later(*tup), inc=True)
        t1b = b.act(lambda s: s.activation(cn2[:, 4:7], cn2[:, 4:7], AF.Exp,
                                           scale=0.5), inc=True)
        t2 = b.dve(lambda v: v.tensor_scalar_add(cn2[:, 4:7], cn2[:, 4:7], 1e-5),
                   waits=[t1b], inc=True)
        b.act(lambda s: s.activation(cn2[:, 4:7], cn2[:, 4:7], AF.Ln),
              waits=[t2])
        t3 = b.act(lambda s: s.activation(cn2[:, 4:7], cn2[:, 4:7], AF.Exp,
                                          scale=-1.0), inc=True)
        t4 = b.dve(lambda v, h=h: v.tensor_mul(scol[:], cn2[:, 4:7],
                                               normc[:, h, :]),
                   waits=[t3], inc=True)
        b.dve(lambda v, h=h: v.tensor_scalar_mul(z0m[:, h, :], znew[:, 0, :],
                                                 scol[:, 0:1]), hard=[t4])
        b.dve(lambda v, h=h: v.tensor_scalar_mul(y1m[:, h, :], znew[:, 1, :],
                                                 scol[:, 1:2]))
        b.dve(lambda v, h=h: v.tensor_scalar_mul(z2m[:, h, :], znew[:, 2, :],
                                                 scol[:, 2:3]))
        if c == 0 and h == 0:
            tap("zn0", znew[:].rearrange("p j x -> p (j x)"), [tup[2]])
        tb0 = b.dve(lambda v: v.tensor_scalar_mul(nsx[:, 0, :], znew[:, 0, :],
                                                  scol[:, 0:1]),
                    waits=later(t_tapp), inc=True)
        tb2 = b.dve(lambda v: v.tensor_scalar_mul(nsx[:, 1, :], znew[:, 2, :],
                                                  scol[:, 2:3]), inc=True)
        tb1 = b.dve(lambda v, h=h: v.tensor_scalar_mul(w1tn[:, h, :],
                                                       znew[:, 1, :],
                                                       scol[:, 1:2]), inc=True)
        w3b = pb_r[3].begin_write()
        b.te(lambda t: t.transpose(PBb[3][:, 0:128], nsx[:, 0, :], ident[:]),
             waits=later(tb0, w3b))
        b.te(lambda t: t.transpose(PBb[3][:, 128:256], nsx[:, 1, :], ident[:]),
             waits=[tb2])
        tt3 = b.te(lambda t, h=h: t.transpose(PBb[3][:, 256:384], w1tn[:, h, :],
                                              ident[:]), waits=[tb1], inc=True)
        pb_r[3].end_write(tt3)
        r3b = pb_r[3].begin_read()
        tws = b.dve(lambda v, h=h: v.tensor_copy(ws_b[:, h, :], PBb[3][:, 0:256]),
                    waits=r3b, inc=True)
        tz1 = b.dve(lambda v, h=h: v.tensor_copy(z1n[:, h, :], PBb[3][:, 256:384]),
                    inc=True)
        pb_r[3].add_reader(tz1)
        if c == 0 and h == 0:
            tap("ws0", ws_b[:, 0, :], [tws])
            tap("z1n0", z1n[:, 0, :], [tz1])
            tap("w1tn0", w1tn[:, 0, :], [tb1])
        if HEAD_CUT != "full":
            return None, tb1
        wc0 = rowC0_r.begin_write()
        tbr = b.dma("gp", lambda g, h=h: g.dma_start(out=rowC0,
                                                     in_=rq_rows[h:h + 1, :]),
                    waits=later(wc0, qtoks["rq"]))
        rowC0_r.end_write(tbr)
        w7 = pb_r[7].begin_write()
        trqb = b.te(lambda t: t.matmul(PB[7][:], ones_r[64:65, :], rowC0,
                                       start=True, stop=True),
                    waits=later(tbr, w7), inc=True)
        pb_r[7].end_write(trqb)
        rowC0_r.add_reader(trqb)
        tqh = b.dve(lambda v, h=h: v.tensor_mul(qhat, qkvs[:, h, :], PB[7][:]),
                    waits=later(pb_r[7].begin_read(), tev[h], tg1), inc=True)
        pb_r[7].add_reader(tqh)
        w0b = pb_r[0].begin_write()
        tgq = b.te(lambda t, h=h: t.matmul(PB[0][:], ws_b[:, h, 0:128], qhat,
                                           start=True, stop=True),
                   waits=later(tqh, tws, w0b), inc=True)
        pb_r[0].end_write(tgq)
        w1b = pb_r[1].begin_write()
        thq = b.te(lambda t, h=h: t.matmul(PB[1][:], ws_b[:, h, 128:256], qhat,
                                           start=True, stop=True),
                   waits=later(w1b), inc=True)
        pb_r[1].end_write(thq)
        tsgq = b.act(lambda s: s.activation(sgq_t, PB[0][:], AF.Silu),
                     waits=later(pb_r[0].begin_read(), tgq), inc=True)
        pb_r[0].add_reader(tsgq)
        tzq = b.dve(lambda v: v.tensor_mul(zq_t, sgq_t, PB[1][:]),
                    waits=later(tsgq, pb_r[1].begin_read(), thq), inc=True)
        pb_r[1].add_reader(tzq)
        w2b = pb_r[2].begin_write()
        toi = b.te(lambda t, h=h: t.matmul(PB[2][:], z1n[:, h, :], zq_t,
                                           start=True, stop=True),
                   waits=later(tzq, tz1, w2b), inc=True)
        pb_r[2].end_write(toi)
        r2b = pb_r[2].begin_read()
        tsq_ = b.act(lambda s: s.activation(sqp[:, 1, :], PB[2][:], AF.Square),
                     waits=later(r2b, toi, sqp_tok[1]), inc=True)
        w6 = pb_r[6].begin_write() if h == 0 else []
        tms = b.te(lambda t, h=h: t.matmul(PB[6][32:48, :], e3b[:, h, :],
                                           sqp[:, 1, :], start=(h == 0),
                                           stop=(h == H - 1)),
                   waits=later(tsq_, w6 if h == 0 else None), inc=True)
        sqp_tok[1] = tms
        if h == H - 1:
            pb_r[6].end_write(tms)
        toin = b.act(lambda s, h=h: s.copy(oins[:, h, :], PB[2][:]), inc=True)
        pb_r[2].add_reader(toin)
        if c == 0 and h == 0:
            tap("qh0", qhat, [tqh])
            tap("zq0", zq_t, [tzq])
            tap("oi0", oins[:, 0, :], [toin])
        return toin, tb1

    def emit_proj(c, oin_toks):
        r6 = pb_r[6].begin_read()
        t1 = b.act(lambda s: s.activation(scr32[32:48, :], PB[6][32:48, :],
                                          AF.Ln, scale=1.0 / 128.0, bias=1e-5),
                   waits=r6, inc=True)
        pb_r[6].add_reader(t1)
        t2 = b.act(lambda s: s.activation(msr, scr32[32:48, :], AF.Exp,
                                          scale=-0.5), inc=True)
        tfix = {}
        for h in range(H):
            wc0 = rowC0_r.begin_write()
            tbr = b.dma("gp", lambda g, h=h: g.dma_start(out=rowC0,
                                                         in_=msr[h:h + 1, :]),
                        waits=later(wc0, t2))
            rowC0_r.end_write(tbr)
            w7 = pb_r[7].begin_write()
            tb_ = b.te(lambda t: t.matmul(PB[7][:], ones_r[64:65, :], rowC0,
                                          start=True, stop=True),
                       waits=later(tbr, w7), inc=True)
            pb_r[7].end_write(tb_)
            rowC0_r.add_reader(tb_)
            tf = b.dve(lambda v, h=h: v.tensor_mul(oins[:, h, :], oins[:, h, :],
                                                   PB[7][:]),
                       waits=later(pb_r[7].begin_read(), oin_toks[h]), inc=True)
            pb_r[7].add_reader(tf)
            tfix[h] = tf
        wsx = xs_r.begin_write()
        tlast = None
        for g in range(D // 256):
            bi, twg = stream_group(wproj_bf, g,
                                   [t_wproj_ready] if c == 0 else None)
            for half in (0, 1):
                ob = 2 * g + half
                bank = ob % 2
                wbk = pb_r[bank].begin_write()
                tmm = mm_acc(PB[bank][:],
                             [(wbuf[bi][:, hh, 128 * half:128 * (half + 1)],
                               oins[:, hh, :]) for hh in range(H)],
                             first_waits=later(twg, wbk,
                                               *[tfix[hh] for hh in range(H)]))
                pb_r[bank].end_write(tmm)
                wb_r[bi].add_reader(tmm)
                tlast = b.dve(lambda v, ob=ob, bank=bank: v.tensor_add(
                    xs[:, ob, :], xs[:, ob, :], PB[bank][:]),
                    waits=later(pb_r[bank].begin_read(), tmm,
                                wsx if ob == 0 else None), inc=True)
                pb_r[bank].add_reader(tlast)
        xs_r.end_write(tlast)
        return tlast

    for c in range(N_CHUNKS):
        load_x(c)
        if "ffn1" in PHASES:
            emit_ln(c, "ln1")
            if c == 0 and "ys1" in taps:
                ys_r.add_reader(tap("ys1", ys[:], ys_r.begin_read()))
            emit_ffn(c, "f1")
            if c == 0 and "ffn1" in taps:
                xs_r.add_reader(tap("ffn1", xs[:], xs_r.begin_read()))
        if "attn" in PHASES:
            emit_ln(c, "ln2")
            if c == 0 and "ys2" in taps:
                ys_r.add_reader(tap("ys2", ys[:], ys_r.begin_read()))
            qtoks = emit_lrqkv(c)
            if c == 0 and "qkv" in taps:
                last = later(*[qtoks["qkv"][j] for j in range(48)])
                tap("qkv", qkvs[:], last)
            nh = {"lrqkv": 0, "h1": 1, "h4": 4, "h16": H}.get(ATTN_CUT, H)
            oin_toks = {}
            prevh = None
            for h in range(nh):
                oin_toks[h], prevh = emit_head(c, h, qtoks, prevh)
            if ATTN_CUT == "full":
                emit_proj(c, oin_toks)
            if c == 0 and "attn" in taps:
                xs_r.add_reader(tap("attn", xs[:], xs_r.begin_read()))
        if "ffn3" in PHASES:
            emit_ln(c, "ln3")
            emit_ffn(c, "f3")
        store_x(c)

    t_end = ("gp", b.n["gp"])
    b.emit("gp", lambda g: g.memset(srow[0:1, 7:8], 0.0), waits=[t_end])

    sems = {e: nc.alloc_semaphore(name=f"sem_{e}") for e in ENGS}

    with nc.Block() as blk:
        def runner(ename):
            def body(eng):
                for item in b.prog[ename]:
                    if item[0] == "w":
                        eng.wait_ge(sems[item[1]], item[2])
                    else:
                        ins = item[1](eng)
                        if item[2]:
                            ins.then_inc(sems[ename], item[3])
            return body
        blk.tensor(runner("te"))
        blk.scalar(runner("act"))
        blk.vector(runner("dve"))
        blk.sync(runner("sp"))
        blk.gpsimd(runner("gp"))

    return nc


_CACHED = {}


def _prep_inputs(inputs):
    x = np.ascontiguousarray(inputs["L0_attn_x"])
    w0 = np.ascontiguousarray(inputs["L0_attn_w0"])
    w1 = np.ascontiguousarray(inputs["L0_attn_w1"])
    w2 = np.ascontiguousarray(inputs["L0_attn_w2"])
    base = {
        "wqkv": np.ascontiguousarray(inputs["L0_attn_W_qkv"]),
        "wlr": np.ascontiguousarray(inputs["L0_attn_W_lr"]),
        "blr": np.ascontiguousarray(inputs["L0_attn_b_lr"]),
        "wproj": np.ascontiguousarray(inputs["L0_attn_W_proj"]),
        "onw": np.ascontiguousarray(inputs["L0_attn_onorm_w"]),
        "w0T": np.ascontiguousarray(w0.transpose(0, 2, 1)),
        "w2T": np.ascontiguousarray(w2.transpose(0, 2, 1)),
        "w1T": np.ascontiguousarray(w1.transpose(0, 2, 1)),
        "wffnT": np.ascontiguousarray(np.asarray(inputs["L0_ffn_W"]).T),
        "bffn": np.ascontiguousarray(inputs["L0_ffn_b"]),
        "lAT": np.ascontiguousarray(
            np.asarray(inputs["L0_ffn_lora_A"]).transpose(2, 0, 1).reshape(D, 128)),
        "lBT": np.ascontiguousarray(
            np.asarray(inputs["L0_ffn_lora_B"]).transpose(0, 2, 1).reshape(128, D)),
        "gwT": np.ascontiguousarray(np.asarray(inputs["L0_ffn_gate_w"]).T),
        "gb": np.ascontiguousarray(inputs["L0_ffn_gate_b"]),
    }
    in_maps = []
    for i in range(8):
        m = dict(base)
        m["xT"] = np.ascontiguousarray(x[i].T)
        in_maps.append(m)
    return in_maps


def kernel(**inputs):
    if "nc" not in _CACHED:
        _CACHED["nc"] = build_nc()
    nc = _CACHED["nc"]
    in_maps = _prep_inputs(inputs)
    res = run_bass_kernel_spmd(nc, in_maps, core_ids=list(range(8)))
    out = np.empty((8, L, D), dtype=np.float32)
    for i in range(8):
        out[i] = res.results[i]["xoutT"].T
    if DEBUG_TAPS:
        kernel.last_taps = [
            {k: v for k, v in r.items() if k.startswith("tap_")}
            for r in res.results]
    return out
